# version 01
# baseline (speedup 1.0000x reference)
import math
import contextlib
import numpy as np
import ml_dtypes
import concourse.bass as bass
import concourse.mybir as mybir
from concourse.bass_utils import run_bass_kernel_spmd

F32 = mybir.dt.float32
BF16 = mybir.dt.bfloat16
AF = mybir.ActivationFunctionType
ALU = mybir.AluOpType
X_AX = mybir.AxisListType.X
BF = ml_dtypes.bfloat16

B, T, D = 4, 4096, 2048
TOK = 2048
NT = TOK // 128
KD = D // 128

ENGINES = ["pe", "act", "dve", "pool", "sp"]


class Tok:
    __slots__ = ("sem", "cnt")

    def __init__(self, sem, cnt):
        self.sem = sem
        self.cnt = cnt


class Emitter:
    NDMA = 16

    def __init__(self, nc):
        self.nc = nc
        self._stack = contextlib.ExitStack()
        self.sems = {}
        self.counts = {}
        self.hwm = {}
        self.ops = {e: [] for e in ENGINES}
        for e in ENGINES:
            self.sems[e] = self._stack.enter_context(nc.semaphore("sem_" + e))
            self.counts[e] = 0
            self.hwm[e] = {}
        self.dsems = [self._stack.enter_context(nc.semaphore(f"dsem{i}"))
                      for i in range(self.NDMA)]
        self.dcounts = [0] * self.NDMA
        self.didx = 0

    def sbuf(self, name, shape, dt):
        self._uid = getattr(self, "_uid", 0) + 1
        return self._stack.enter_context(self.nc.sbuf_tensor(f"{name}_s{self._uid}", shape, dt))

    def psum(self, name, shape, dt=F32):
        self._uid = getattr(self, "_uid", 0) + 1
        return self._stack.enter_context(self.nc.psum_tensor(f"{name}_p{self._uid}", shape, dt))

    def _emit_waits(self, engine, deps):
        for d in deps:
            if d is None:
                continue
            key = id(d.sem)
            if self.hwm[engine].get(key, -1) >= d.cnt:
                continue
            self.hwm[engine][key] = d.cnt
            sem, cnt = d.sem, d.cnt
            self.ops[engine].append(lambda eng, sem=sem, cnt=cnt: eng.wait_ge(sem, cnt))

    def op(self, engine, fn, deps=(), inc=1):
        self._emit_waits(engine, deps)
        sem = self.sems[engine]
        self.counts[engine] += inc
        cnt = self.counts[engine]

        def run(eng, fn=fn, sem=sem, inc=inc):
            inst = fn(eng)
            if inst is not None:
                inst.then_inc(sem, inc)
            else:
                eng.nop().then_inc(sem, inc)
        self.ops[engine].append(run)
        return Tok(sem, cnt)

    def ops_seq(self, engine, fns, deps=()):
        self._emit_waits(engine, deps)
        sem = self.sems[engine]
        self.counts[engine] += 1
        cnt = self.counts[engine]
        n = len(fns)
        if n == 0:
            self.ops[engine].append(lambda eng, sem=sem: eng.nop().then_inc(sem, 1))
            return Tok(sem, cnt)
        for i, fn in enumerate(fns):
            last = i == n - 1

            def run(eng, fn=fn, last=last, sem=sem):
                inst = fn(eng)
                if last:
                    if inst is not None:
                        inst.then_inc(sem, 1)
                    else:
                        eng.nop().then_inc(sem, 1)
            self.ops[engine].append(run)
        return Tok(sem, cnt)

    def dma(self, engine, fn, deps=()):
        i = self.didx
        self.didx = (self.didx + 1) % self.NDMA
        sem = self.dsems[i]
        prev_cnt = self.dcounts[i]
        deps = list(deps) + ([Tok(sem, prev_cnt)] if prev_cnt else [])
        self._emit_waits(engine, deps)
        self.dcounts[i] += 16
        cnt = self.dcounts[i]

        def run(eng, fn=fn, sem=sem):
            fn(eng).then_inc(sem, 16)
        self.ops[engine].append(run)
        return Tok(sem, cnt)

    def tok(self, engine):
        return Tok(self.sems[engine], self.counts[engine])

    def finalize(self):
        nc = self.nc
        block = self._stack.enter_context(nc.Block())
        em = self

        @block.tensor
        def _(eng):
            for f in em.ops["pe"]:
                f(eng)

        @block.scalar
        def _(eng):
            for f in em.ops["act"]:
                f(eng)

        @block.vector
        def _(eng):
            for f in em.ops["dve"]:
                f(eng)

        @block.gpsimd
        def _(eng):
            for f in em.ops["pool"]:
                f(eng)

        @block.sync
        def _(eng):
            for f in em.ops["sp"]:
                f(eng)

        self._stack.close()



def emit_ln(em, x_in, lnf, ids, ids_tok, ntiles=NT, tr_ps=None, guard=None):
    if not hasattr(em, "_ln_scratch"):
        em._ln_scratch = dict(
            xs=em.sbuf("ln_xs", [128, 2 * D], F32),
            sq=em.sbuf("ln_sqd", [128, D], BF16),
            ssum=em.sbuf("ln_ssum", [128, 1], F32),
            sqs=em.sbuf("ln_sqs", [128, 1], F32),
            var=em.sbuf("ln_var", [128, 1], F32),
            msq=em.sbuf("ln_msq", [128, 1], F32),
            std=em.sbuf("ln_std", [128, 1], F32),
            mcols=em.sbuf("ln_m", [128, NT], F32),
            rcols=em.sbuf("ln_r", [128, NT], F32),
            lnt=em.sbuf("ln_lnt", [128, 2 * D], BF16),
        )
    _s = em._ln_scratch
    xs = _s["xs"]; sq = _s["sq"]; ssum = _s["ssum"]; sqs = _s["sqs"]
    var = _s["var"]; msq = _s["msq"]; std = _s["std"]
    mcols = _s["mcols"]; rcols = _s["rcols"]; lnt = _s["lnt"]
    lnf_v = lnf[:].rearrange("p (k it) -> p k it", k=KD)
    stats = []
    out_toks = []
    ev_hist = {}
    last_reader = {0: [], 1: []}
    prev_lnt = {0: None, 1: None}
    prev_stats = None
    for i in range(ntiles):
        lb = i % 2
        t_x = em.dma("sp", lambda e, i=i, lb=lb: e.dma_start(
            out=xs[:, lb * D:(lb + 1) * D], in_=x_in[i * 128:(i + 1) * 128, :]),
            deps=last_reader[lb])
        xt = xs[:, lb * D:(lb + 1) * D]
        sdep = [t_x] + ([prev_stats] if prev_stats else [])
        t_sq = em.op("act", lambda e, xt=xt: e.activation(sq[:], xt, AF.Square, accum_out=sqs[:]), deps=sdep)
        t_sum = em.op("dve", lambda e, xt=xt: e.reduce_sum(ssum[:], xt, axis=X_AX), deps=sdep)
        mcol = mcols[:, i:i + 1]
        rcol = rcols[:, i:i + 1]
        t_m = em.op("dve", lambda e, mcol=mcol: e.tensor_scalar_mul(mcol, ssum[:], 1.0 / D), deps=[t_sum])
        t_msq = em.op("dve", lambda e, mcol=mcol: e.tensor_mul(msq[:], mcol, mcol), deps=[t_m])
        t_var = em.op("dve", lambda e: e.scalar_tensor_tensor(var[:], sqs[:], 1.0 / D, msq[:],
                                                             ALU.mult, ALU.subtract), deps=[t_msq, t_sq])
        t_ve = em.op("dve", lambda e: e.tensor_scalar_add(var[:], var[:], 1e-5), deps=[t_var])
        t_std = em.op("act", lambda e: e.activation(std[:], var[:], AF.Sqrt), deps=[t_ve])
        t_r = em.op("dve", lambda e, rcol=rcol: e.reciprocal(rcol, std[:]), deps=[t_std])
        prev_stats = t_r
        lnt_t = lnt[:, lb * D:(lb + 1) * D]
        t_ln = em.op("dve", lambda e, xt=xt, lnt_t=lnt_t, mcol=mcol, rcol=rcol:
                     e.tensor_scalar(lnt_t, xt, mcol, rcol, ALU.subtract, ALU.mult),
                     deps=[t_r] + ([prev_lnt[lb]] if prev_lnt[lb] else []))
        last_reader[lb] = [t_sq, t_ln]
        stats.append((mcol, rcol))
        tile_evs = []
        for half in range(2):
            pb = tr_ps[half]
            bank_guard = ev_hist.get((i - 1, half), guard)
            t_tr = em.ops_seq("pe", [
                (lambda e, lnt_t=lnt_t, k=half * 8 + kk, kk=kk, pb=pb:
                 e.transpose(pb[:, kk * 128:(kk + 1) * 128], lnt_t[:, k * 128:(k + 1) * 128], ids[:]))
                for kk in range(8)
            ], deps=[t_ln, ids_tok] + ([bank_guard] if bank_guard else []))
            t_ev = em.op("dve", lambda e, pb=pb, i=i, half=half:
                         e.tensor_copy(lnf_v[:, half * 8:(half + 1) * 8, i * 128:(i + 1) * 128],
                                       pb[:].rearrange("p (k t) -> p k t", k=8)),
                         deps=[t_tr])
            ev_hist[(i, half)] = t_ev
            tile_evs.append(t_ev)
        prev_lnt[lb] = tile_evs[1]
        out_toks.extend(tile_evs)
    return stats, out_toks


def np_bf16(a):
    return np.ascontiguousarray(np.asarray(a)).astype(BF)


IDENT = np.eye(128, dtype=BF)



def build_ffn_l0():
    nc = bass.Bass("TRN2", target_bir_lowering=False)
    x_in = nc.dram_tensor("x", [TOK, D], F32, kind="ExternalInput").ap()
    wrt_in = nc.dram_tensor("wrt", [D, 32], BF16, kind="ExternalInput").ap()
    ut_in = nc.dram_tensor("ut", [D, 512], BF16, kind="ExternalInput").ap()
    vf_in = nc.dram_tensor("vf", [512, D], BF16, kind="ExternalInput").ap()
    attrt_in = nc.dram_tensor("attrt", [32, 32], BF16, kind="ExternalInput").ap()
    wat_in = nc.dram_tensor("wat", [32, 1], BF16, kind="ExternalInput").ap()
    sel_in = nc.dram_tensor("sel", [32, 2048], BF16, kind="ExternalInput").ap()
    id_in = nc.dram_tensor("ident", [128, 128], BF16, kind="ExternalInput").ap()
    xo = nc.dram_tensor("xo", [TOK, D], F32, kind="ExternalOutput").ap()

    em = Emitter(nc)
    xs = em.sbuf("xres", [128, 2 * D], F32)
    lnf = em.sbuf("lnf", [128, KD * TOK], BF16)
    ids = em.sbuf("ids", [128, 128], BF16)
    wrt = em.sbuf("wrt", [128, KD * 32], BF16)
    ut = em.sbuf("ut", [128, KD * 512], BF16)
    vf = em.sbuf("vf", [128, 4 * D], BF16)
    attrt = em.sbuf("attrt", [32, 32], BF16)
    wat = em.sbuf("wat", [32, 1], BF16)
    sel = em.sbuf("sel", [32, 2048], BF16)
    aw_row = em.sbuf("aw_row", [1, 32], BF16)
    awb_sb = em.sbuf("awb_sb", [128, 32], F32)
    onecol = em.sbuf("onecol", [1, 128], BF16)

    trp = [em.psum(f"trp{j}", [128, 1024], BF16) for j in range(2)]
    sc_ps = em.psum("sc_ps", [128, 512], F32)
    hu_ps = [em.psum(f"hu_ps{j}", [128, 512], F32) for j in range(2)]
    gb_ps = em.psum("gb_ps", [128, 512], F32)
    out_ps = [em.psum(f"out_ps{j}", [128, 512], F32) for j in range(2)]

    t_ids = em.dma("sp", lambda e: e.dma_start(out=ids[:], in_=id_in))
    t_wrt = em.dma("sp", lambda e: e.dma_start(
        out=wrt[:].rearrange("p (k o) -> p k o", k=KD), in_=wrt_in.rearrange("(k p) o -> p k o", p=128)))
    t_ut = em.dma("sp", lambda e: e.dma_start(
        out=ut[:].rearrange("p (k o) -> p k o", k=KD), in_=ut_in.rearrange("(k p) o -> p k o", p=128)))
    t_vf = em.dma("sp", lambda e: e.dma_start(
        out=vf[:].rearrange("p (k o) -> p k o", k=4), in_=vf_in.rearrange("(k p) o -> p k o", p=128)))
    t_attr = em.dma("sp", lambda e: e.dma_start(out=attrt[:], in_=attrt_in))
    t_wat = em.dma("sp", lambda e: e.dma_start(out=wat[:], in_=wat_in))
    t_sel = em.dma("sp", lambda e: e.dma_start(out=sel[:], in_=sel_in))

    stats, ln_toks = emit_ln(em, x_in, lnf, ids, t_ids, tr_ps=trp)

    t_awm = em.op("pe", lambda e: e.matmul(sc_ps[0:1, 0:32], wat[:], attrt[:], start=True, stop=True),
                  deps=[t_attr, t_wat])
    t_aw = em.op("act", lambda e: e.activation(aw_row[:], sc_ps[0:1, 0:32], AF.Sigmoid), deps=[t_awm])
    t_onec = em.op("dve", lambda e: e.memset(onecol[:], 1.0))
    t_awb = em.op("pe", lambda e: e.matmul(sc_ps[:, 0:32], onecol[:], aw_row[:], start=True, stop=True),
                  deps=[t_aw, t_onec])
    t_awbs = em.op("dve", lambda e: e.tensor_copy(awb_sb[:], sc_ps[:, 0:32]), deps=[t_awb])

    gbuf = em.sbuf("gbuf", [128, NT * 32], BF16)
    fcol = em.sbuf("fcol", [128, NT], F32)
    nm = em.sbuf("nm", [128, 1], F32)
    probs = em.sbuf("probs", [128, 32], F32)
    prsum = em.sbuf("prsum", [128, 1], F32)
    m1 = em.sbuf("m1", [128, 1], F32)
    eq1 = em.sbuf("eq1", [128, 32], F32)
    masked = em.sbuf("masked", [128, 32], F32)
    m2 = em.sbuf("m2", [128, 1], F32)
    keep = em.sbuf("keep", [128, 32], F32)
    gw = em.sbuf("gw", [128, 32], F32)
    gsum = em.sbuf("gsum", [128, 1], F32)
    grecip = em.sbuf("grecip", [128, 1], F32)
    gate = em.sbuf("gate", [128, 1], F32)

    g_toks = []
    prev_g = t_awbs
    for i in range(NT):
        t_mm = em.ops_seq("pe", [
            (lambda e, k=k, i=i: e.matmul(sc_ps[:, 0:32],
                                          lnf[:, (k * NT + i) * 128:(k * NT + i + 1) * 128],
                                          wrt[:, k * 32:(k + 1) * 32],
                                          start=(k == 0), stop=(k == KD - 1)))
            for k in range(KD)
        ], deps=ln_toks + [t_wrt, prev_g])
        t_nm = em.op("dve", lambda e: e.reduce_max(nm[:], sc_ps[:, 0:32], axis=X_AX, negate=True),
                     deps=[t_mm, prev_g])
        t_e = em.op("act", lambda e: e.activation(probs[:], sc_ps[:, 0:32], AF.Exp, bias=nm[:],
                                                  accum_out=prsum[:]), deps=[t_nm])
        t_m1 = em.op("dve", lambda e: e.reduce_max(m1[:], probs[:], axis=X_AX), deps=[t_e])
        t_eq = em.op("dve", lambda e: e.tensor_scalar(eq1[:], probs[:], m1[:], None, ALU.is_ge), deps=[t_m1])
        t_mk = em.op("dve", lambda e: e.scalar_tensor_tensor(masked[:], eq1[:], -1e30, probs[:],
                                                             ALU.mult, ALU.add), deps=[t_eq])
        t_m2 = em.op("dve", lambda e: e.reduce_max(m2[:], masked[:], axis=X_AX), deps=[t_mk])
        t_kp = em.op("dve", lambda e: e.tensor_scalar(keep[:], probs[:], m2[:], None, ALU.is_ge), deps=[t_m2])
        t_gw = em.op("dve", lambda e: e.tensor_mul(gw[:], probs[:], keep[:]), deps=[t_kp])
        t_gs = em.op("dve", lambda e: e.reduce_sum(gsum[:], gw[:], axis=X_AX), deps=[t_gw])
        t_ge = em.op("dve", lambda e: e.scalar_tensor_tensor(gsum[:], prsum[:], 1e-8, gsum[:],
                                                             ALU.mult, ALU.add), deps=[t_gs])
        t_gr = em.op("dve", lambda e: e.reciprocal(grecip[:], gsum[:]), deps=[t_ge])
        gslice = gbuf[:, i * 32:(i + 1) * 32]
        t_g = em.op("dve", lambda e, gslice=gslice: e.tensor_scalar(gslice, gw[:], grecip[:], None, ALU.mult),
                    deps=[t_gr])
        t_gt = em.op("dve", lambda e, gslice=gslice: e.tensor_tensor(gw[:], gslice, awb_sb[:], ALU.mult),
                     deps=[t_g, t_awbs])
        t_gt2 = em.op("dve", lambda e: e.reduce_sum(gate[:], gw[:], axis=X_AX), deps=[t_gt])
        fc = fcol[:, i:i + 1]
        t_f = em.op("dve", lambda e, fc=fc: e.tensor_scalar(fc, gate[:], 0.1, 0.45, ALU.mult, ALU.add),
                    deps=[t_gt2])
        prev_g = t_f
        g_toks.append((t_g, t_f))

    gT = em.sbuf("gT", [32, TOK], BF16)
    gT_toks = []
    ln_done = ln_toks[-1]
    for i in range(NT):
        guard = gT_toks[i - 1] if i >= 1 else ln_done
        t_tr = em.op("pe", lambda e, i=i: e.transpose(trp[0][0:32, (i % 8) * 128:(i % 8 + 1) * 128],
                                                      gbuf[:, i * 32:(i + 1) * 32], ids[:]),
                     deps=[g_toks[i][0], guard])
        t_ev = em.op("dve", lambda e, i=i: e.tensor_copy(gT[:, i * 128:(i + 1) * 128],
                                                         trp[0][0:32, (i % 8) * 128:(i % 8 + 1) * 128]),
                     deps=[t_tr])
        gT_toks.append(t_ev)

    gated = em.sbuf("gated", [128, 4 * TOK], BF16)
    gbc_sb = em.sbuf("gbc_sb", [128, 512], F32)
    gated_toks = {}
    prev_hu = {0: None, 1: None}
    prev_gb = None
    prev_gbc = None
    for sc_i in range(4):
        for ts in range(TOK // 512):
            pb = (sc_i * 4 + ts) % 2
            hdeps = ln_toks + [t_ut] + ([prev_hu[pb]] if prev_hu[pb] else [])
            t_hu = em.ops_seq("pe", [
                (lambda e, k=k, sc_i=sc_i, ts=ts, pb=pb: e.matmul(
                    hu_ps[pb][:],
                    ut[:, k * 512 + sc_i * 128:k * 512 + (sc_i + 1) * 128],
                    lnf[:, (k * NT + ts * 4) * 128:(k * NT + (ts + 1) * 4) * 128],
                    start=(k == 0), stop=(k == KD - 1)))
                for k in range(KD)
            ], deps=hdeps)
            t_gb = em.op("pe", lambda e, sc_i=sc_i, ts=ts: e.matmul(
                gb_ps[:], sel[:, sc_i * 512:sc_i * 512 + 128], gT[:, ts * 512:(ts + 1) * 512],
                start=True, stop=True),
                deps=[gT_toks[ts * 4 + 3], t_sel] + ([prev_gb] if prev_gb else []))
            t_gbc = em.op("dve", lambda e: e.tensor_copy(gbc_sb[:], gb_ps[:]),
                          deps=[t_gb] + ([prev_gbc] if prev_gbc else []))
            prev_gb = t_gbc
            gd = gated[:, sc_i * TOK + ts * 512: sc_i * TOK + (ts + 1) * 512]
            t_gd = em.op("dve", lambda e, gd=gd, pb=pb: e.tensor_tensor(gd, hu_ps[pb][:], gbc_sb[:], ALU.mult),
                         deps=[t_hu, t_gbc])
            prev_hu[pb] = t_gd
            prev_gbc = t_gd
            gated_toks[(sc_i, ts)] = t_gd

    st_toks = []
    prev_out = {0: None, 1: None}
    prev_store = {0: None, 1: None}
    for i in range(NT):
        lb = i % 2
        t_x = em.dma("sp", lambda e, i=i, lb=lb: e.dma_start(
            out=xs[:, lb * D:(lb + 1) * D], in_=x_in[i * 128:(i + 1) * 128, :]),
            deps=[prev_store[lb]] if prev_store[lb] else [])
        tile_res = []
        for sl in range(4):
            pb = (i * 4 + sl) % 2
            deps = [gated_toks[(sc_i, i // 4)] for sc_i in range(4)] + [t_vf]
            if prev_out[pb]:
                deps.append(prev_out[pb])
            t_mm = em.ops_seq("pe", [
                (lambda e, sc_i=sc_i, i=i, sl=sl, pb=pb: e.matmul(
                    out_ps[pb][:],
                    gated[:, sc_i * TOK + i * 128: sc_i * TOK + (i + 1) * 128],
                    vf[:, sc_i * D + sl * 512: sc_i * D + (sl + 1) * 512],
                    start=(sc_i == 0), stop=(sc_i == 3)))
                for sc_i in range(4)
            ], deps=deps)
            xt = xs[:, lb * D + sl * 512: lb * D + (sl + 1) * 512]
            fc = fcol[:, i:i + 1]
            t_res = em.op("dve", lambda e, xt=xt, fc=fc, pb=pb: e.scalar_tensor_tensor(
                xt, out_ps[pb][:], fc, xt, ALU.mult, ALU.add), deps=[t_mm, g_toks[i][1], t_x])
            prev_out[pb] = t_res
            tile_res.append(t_res)
        t_st = em.dma("sp", lambda e, i=i, lb=lb: e.dma_start(
            out=xo[i * 128:(i + 1) * 128, :], in_=xs[:, lb * D:(lb + 1) * D]),
            deps=tile_res)
        prev_store[lb] = t_st
        st_toks.append(t_st)
    em.op("sp", lambda e: None, deps=st_toks)
    em.finalize()
    return nc



def prep_l0ffn(inp):
    sel = np.zeros((32, 2048), dtype=np.float32)
    for sc_i in range(4):
        for p in range(128):
            s = (sc_i * 128 + p) // 16
            sel[s, sc_i * 512 + p] = 1.0
    return {
        "wrt": np_bf16(np.asarray(inp["L0_ffn_Wr"]).T),
        "ut": np_bf16(np.transpose(np.asarray(inp["L0_ffn_U"]), (1, 0, 2)).reshape(2048, 512)),
        "vf": np_bf16(np.asarray(inp["L0_ffn_V"]).reshape(512, 2048)),
        "attrt": np_bf16(np.asarray(inp["L0_ffn_attr"]).T),
        "wat": np_bf16(np.asarray(inp["L0_ffn_Wa"]).T),
        "sel": np_bf16(sel),
        "ident": IDENT,
    }





def build_qkv():
    nc = bass.Bass("TRN2", target_bir_lowering=False)
    x_in = nc.dram_tensor("x", [TOK, D], F32, kind="ExternalInput").ap()
    wq_in = nc.dram_tensor("wq", [D, D], BF16, kind="ExternalInput").ap()
    wk_in = nc.dram_tensor("wk", [D, 512], BF16, kind="ExternalInput").ap()
    wv_in = nc.dram_tensor("wv", [D, 512], BF16, kind="ExternalInput").ap()
    id_in = nc.dram_tensor("ident", [128, 128], BF16, kind="ExternalInput").ap()
    q_out = nc.dram_tensor("q", [D, TOK], BF16, kind="ExternalOutput").ap()
    k_out = nc.dram_tensor("k", [512, TOK], BF16, kind="ExternalOutput").ap()
    v_out = nc.dram_tensor("v", [TOK, 512], BF16, kind="ExternalOutput").ap()

    em = Emitter(nc)
    lnf = em.sbuf("lnf", [128, KD * TOK], BF16)
    ids = em.sbuf("ids", [128, 128], BF16)
    wq = em.sbuf("wq", [128, KD * D], BF16)
    wkv = em.sbuf("wkv", [128, KD * 1024], BF16)
    stage = em.sbuf("stage", [128, 2 * 512], BF16)
    trp = [em.psum(f"trp{j}", [128, 1024], BF16) for j in range(2)]
    pp = [em.psum(f"pp{j}", [128, 512], F32) for j in range(4)]

    t_ids = em.dma("sp", lambda e: e.dma_start(out=ids[:], in_=id_in))
    t_wq = em.dma("sp", lambda e: e.dma_start(
        out=wq[:].rearrange("p (k o) -> p k o", k=KD), in_=wq_in.rearrange("(k p) o -> p k o", p=128)))
    t_wk = em.dma("sp", lambda e: e.dma_start(
        out=wkv[:].rearrange("p (k o) -> p k o", k=KD)[:, :, 0:512],
        in_=wk_in.rearrange("(k p) o -> p k o", p=128)))
    t_wv = em.dma("sp", lambda e: e.dma_start(
        out=wkv[:].rearrange("p (k o) -> p k o", k=KD)[:, :, 512:1024],
        in_=wv_in.rearrange("(k p) o -> p k o", p=128)))

    stats, ln_toks = emit_ln(em, x_in, lnf, ids, t_ids, tr_ps=trp)
    SC = 1.0 / math.sqrt(128.0)
    all_stores = []

    prev = {}
    for oc in range(KD):
        for ts in range(TOK // 512):
            pb = (oc * 4 + ts) % 4
            t_mm = em.ops_seq("pe", [
                (lambda e, k=k, oc=oc, ts=ts, pb=pb: e.matmul(
                    pp[pb][:], wq[:, (k * KD + oc) * 128:(k * KD + oc + 1) * 128],
                    lnf[:, (k * NT + ts * 4) * 128:(k * NT + (ts + 1) * 4) * 128],
                    start=(k == 0), stop=(k == KD - 1)))
                for k in range(KD)
            ], deps=ln_toks + [t_wq] + ([prev.get(pb)] if pb in prev else []))
            sb = stage[:, (pb % 2) * 512:(pb % 2 + 1) * 512]
            t_ev = em.op("dve", lambda e, sb=sb, pb=pb: e.tensor_scalar_mul(sb, pp[pb][:], SC),
                         deps=[t_mm] + ([prev.get(("s", pb % 2))] if ("s", pb % 2) in prev else []))
            t_st = em.dma("sp", lambda e, sb=sb, oc=oc, ts=ts: e.dma_start(
                out=q_out[oc * 128:(oc + 1) * 128, ts * 512:(ts + 1) * 512], in_=sb), deps=[t_ev])
            prev[pb] = t_ev
            prev[("s", pb % 2)] = t_st
            all_stores.append(t_st)
    for oc in range(4):
        for ts in range(TOK // 512):
            pb = (oc * 4 + ts) % 4
            t_mm = em.ops_seq("pe", [
                (lambda e, k=k, oc=oc, ts=ts, pb=pb: e.matmul(
                    pp[pb][:], wkv[:, k * 1024 + oc * 128:k * 1024 + (oc + 1) * 128],
                    lnf[:, (k * NT + ts * 4) * 128:(k * NT + (ts + 1) * 4) * 128],
                    start=(k == 0), stop=(k == KD - 1)))
                for k in range(KD)
            ], deps=ln_toks + [t_wk] + ([prev.get(pb)] if pb in prev else []))
            sb = stage[:, (pb % 2) * 512:(pb % 2 + 1) * 512]
            t_ev = em.op("dve", lambda e, sb=sb, pb=pb: e.tensor_copy(sb, pp[pb][:]),
                         deps=[t_mm] + ([prev.get(("s", pb % 2))] if ("s", pb % 2) in prev else []))
            t_st = em.dma("sp", lambda e, sb=sb, oc=oc, ts=ts: e.dma_start(
                out=k_out[oc * 128:(oc + 1) * 128, ts * 512:(ts + 1) * 512], in_=sb), deps=[t_ev])
            prev[pb] = t_ev
            prev[("s", pb % 2)] = t_st
            all_stores.append(t_st)
    for i in range(NT):
        pb = i % 4
        t_mm = em.ops_seq("pe", [
            (lambda e, k=k, i=i, pb=pb: e.matmul(
                pp[pb][:], lnf[:, (k * NT + i) * 128:(k * NT + i + 1) * 128],
                wkv[:, k * 1024 + 512:(k + 1) * 1024],
                start=(k == 0), stop=(k == KD - 1)))
            for k in range(KD)
        ], deps=ln_toks + [t_wv] + ([prev.get(pb)] if pb in prev else []))
        sb = stage[:, (pb % 2) * 512:(pb % 2 + 1) * 512]
        t_ev = em.op("dve", lambda e, sb=sb, pb=pb: e.tensor_copy(sb, pp[pb][:]),
                     deps=[t_mm] + ([prev.get(("s", pb % 2))] if ("s", pb % 2) in prev else []))
        t_st = em.dma("sp", lambda e, sb=sb, i=i: e.dma_start(
            out=v_out[i * 128:(i + 1) * 128, :], in_=sb), deps=[t_ev])
        prev[pb] = t_ev
        prev[("s", pb % 2)] = t_st
        all_stores.append(t_st)
    em.op("sp", lambda e: None, deps=all_stores)
    em.finalize()
    return nc


NQB = T // 128
SEGB = 8


def build_attn():
    nc = bass.Bass("TRN2", target_bir_lowering=False)
    q_in = nc.dram_tensor("q", [1024, T], BF16, kind="ExternalInput").ap()
    k_in = nc.dram_tensor("k", [256, T], BF16, kind="ExternalInput").ap()
    v_in = nc.dram_tensor("v", [T, 256], BF16, kind="ExternalInput").ap()
    wo_in = nc.dram_tensor("wo", [1024, D], BF16, kind="ExternalInput").ap()
    id_in = nc.dram_tensor("ident", [128, 128], BF16, kind="ExternalInput").ap()
    tm_in = nc.dram_tensor("trimask", [128, 128], F32, kind="ExternalInput").ap()
    p_out = nc.dram_tensor("part", [T, D], BF16, kind="ExternalOutput").ap()

    em = Emitter(nc)
    kf = em.sbuf("kf", [128, 2 * T], BF16)
    vt = em.sbuf("vt", [128, (T // 128) * 256], BF16)
    wo = em.sbuf("wo", [128, 8 * D], BF16)
    ids = em.sbuf("ids", [128, 128], BF16)
    tm = em.sbuf("tm", [128, 128], F32)
    qf = em.sbuf("qf", [128, 2 * 8 * 128], BF16)
    attn_sb = em.sbuf("attn_sb", [128, T], BF16)
    attnT_sb = em.sbuf("attnT_sb", [128, 1024], BF16)
    ctx_sb = em.sbuf("ctx_sb", [128, 1024], BF16)
    ctxT_sb = em.sbuf("ctxT_sb", [128, 1024], BF16)
    part_sb = em.sbuf("part_sb", [128, 2 * 512], BF16)
    nmr = em.sbuf("nmr", [128, 3], F32)
    smr = em.sbuf("smr", [128, 2], F32)
    rec = em.sbuf("rec", [128, 8], F32)

    sc_ps = em.psum("sc_ps", [128, 1024], F32)
    at_ps = [em.psum(f"at_ps{j}", [128, 1024], BF16) for j in range(2)]
    ctx_ps = em.psum("ctx_ps", [128, 1024], F32)
    po_ps = [em.psum(f"po_ps{j}", [128, 512], F32) for j in range(2)]

    t_ids = em.dma("sp", lambda e: e.dma_start(out=ids[:], in_=id_in))
    t_tm = em.dma("sp", lambda e: e.dma_start(out=tm[:], in_=tm_in))
    t_kf = em.dma("sp", lambda e: e.dma_start(
        out=kf[:].rearrange("p (g t) -> p g t", g=2), in_=k_in.rearrange("(g p) t -> p g t", p=128)))
    t_vt = em.dma("sp", lambda e: e.dma_start(
        out=vt[:].rearrange("p (i c) -> p i c", c=256), in_=v_in.rearrange("(i p) c -> p i c", p=128)))
    t_wo = em.dma("sp", lambda e: e.dma_start(
        out=wo[:].rearrange("p (c o) -> p c o", c=8), in_=wo_in.rearrange("(c p) o -> p c o", p=128)))

    prev_q = {0: None, 1: None}
    prev_sc = None
    prev_at = {0: None, 1: None}
    prev_ctxev = None
    prev_po = {0: None, 1: None}
    prev_pstg = {0: None, 1: None}
    prev_ctxT = None
    attn_stores = []
    for j in range(NQB):
        qb = j % 2
        t_q = em.dma("sp", lambda e, j=j, qb=qb: e.dma_start(
            out=qf[:, qb * 1024:(qb + 1) * 1024].rearrange("p (h t) -> p h t", h=8),
            in_=q_in[:, j * 128:(j + 1) * 128].rearrange("(h p) t -> p h t", p=128)),
            deps=[prev_q[qb]] if prev_q[qb] else [])
        nkb = j + 1
        nseg = (nkb + SEGB - 1) // SEGB
        head_done = []
        for hh in range(8):
            g = hh // 4
            qh = qf[:, qb * 1024 + hh * 128: qb * 1024 + (hh + 1) * 128]
            for s in range(nseg):
                b0 = s * SEGB
                nb = min(SEGB, nkb - b0)
                nmm = (nb + 3) // 4
                t_mm = em.ops_seq("pe", [
                    (lambda e, m=m, b0=b0, nb=nb, g=g, qh=qh: e.matmul(
                        sc_ps[:, m * 512: m * 512 + min(512, (nb - m * 4) * 128)],
                        qh, kf[:, g * T + (b0 + m * 4) * 128: g * T + (b0 + m * 4) * 128 + min(512, (nb - m * 4) * 128)],
                        start=True, stop=True))
                    for m in range(nmm)
                ], deps=[t_q, t_kf] + ([prev_sc] if prev_sc else []))
                if s == 0:
                    t_nm = em.op("dve", lambda e, nb=nb: e.reduce_max(
                        nmr[:, 0:1], sc_ps[:, 0:nb * 128], axis=X_AX, negate=True), deps=[t_mm])
                else:
                    t_nm1 = em.op("dve", lambda e, nb=nb: e.reduce_max(
                        nmr[:, 1:2], sc_ps[:, 0:nb * 128], axis=X_AX, negate=True), deps=[t_mm])
                    t_nm = em.op("dve", lambda e: e.tensor_tensor(
                        nmr[:, 0:1], nmr[:, 0:1], nmr[:, 1:2], ALU.min), deps=[t_nm1])
                prev_sc = t_nm
            first_pv = True
            for s in range(nseg):
                b0 = s * SEGB
                nb = min(SEGB, nkb - b0)
                nmm = (nb + 3) // 4
                t_mm = em.ops_seq("pe", [
                    (lambda e, m=m, b0=b0, nb=nb, g=g, qh=qh: e.matmul(
                        sc_ps[:, m * 512: m * 512 + min(512, (nb - m * 4) * 128)],
                        qh, kf[:, g * T + (b0 + m * 4) * 128: g * T + (b0 + m * 4) * 128 + min(512, (nb - m * 4) * 128)],
                        start=True, stop=True))
                    for m in range(nmm)
                ], deps=[t_q, t_kf] + ([prev_sc] if prev_sc else []))
                has_diag = (b0 + nb - 1) == j
                if has_diag:
                    t_mask = em.op("dve", lambda e, nb=nb: e.tensor_tensor(
                        sc_ps[:, (nb - 1) * 128:nb * 128], sc_ps[:, (nb - 1) * 128:nb * 128],
                        tm[:], ALU.add), deps=[t_mm, t_tm])
                else:
                    t_mask = t_mm
                asl = attn_sb[:, b0 * 128:(b0 + nb) * 128]
                t_e = em.op("act", lambda e, asl=asl, nb=nb: e.activation(
                    asl, sc_ps[:, 0:nb * 128], AF.Exp, bias=nmr[:, 0:1], accum_out=smr[:, 1:2]),
                    deps=[t_mask])
                prev_sc = t_e
                if s == 0:
                    t_sum = em.op("dve", lambda e: e.tensor_copy(smr[:, 0:1], smr[:, 1:2]), deps=[t_e])
                else:
                    t_sum = em.op("dve", lambda e: e.tensor_tensor(
                        smr[:, 0:1], smr[:, 0:1], smr[:, 1:2], ALU.add), deps=[t_e])
                for bi in range(nb):
                    kb = b0 + bi
                    ab = kb % 2
                    reg = (kb // 2) % 4
                    t_tr = em.op("pe", lambda e, kb=kb, ab=ab, reg=reg: e.transpose(
                        at_ps[ab][:, reg * 256 + 0:reg * 256 + 128] if False else
                        at_ps[ab][:, (kb % 8) * 128:(kb % 8 + 1) * 128],
                        attn_sb[:, kb * 128:(kb + 1) * 128], ids[:]),
                        deps=[t_e, t_ids] + ([prev_at[ab]] if prev_at[ab] else []))
                    t_aev = em.op("dve", lambda e, kb=kb, ab=ab: e.tensor_copy(
                        attnT_sb[:, (kb % 8) * 128:(kb % 8 + 1) * 128],
                        at_ps[ab][:, (kb % 8) * 128:(kb % 8 + 1) * 128]), deps=[t_tr])
                    prev_at[ab] = t_aev
                    t_pv = em.op("pe", lambda e, kb=kb, hh=hh, g=g, first_pv=first_pv: e.matmul(
                        ctx_ps[:, hh * 128:(hh + 1) * 128],
                        attnT_sb[:, (kb % 8) * 128:(kb % 8 + 1) * 128],
                        vt[:].rearrange("p (i c) -> p i c", c=256)[:, kb, g * 128:(g + 1) * 128],
                        start=first_pv, stop=(kb == j)), deps=[t_aev, t_vt] + ([prev_ctxev] if prev_ctxev else []))
                    first_pv = False
            t_rec = em.op("dve", lambda e, hh=hh: e.reciprocal(rec[:, hh:hh + 1], smr[:, 0:1]), deps=[t_sum])
            head_done.append((t_pv, t_rec))
        all_pv = [hd[0] for hd in head_done] + [hd[1] for hd in head_done]
        t_cev = None
        for hh in range(8):
            t_cev = em.op("act", lambda e, hh=hh: e.activation(
                ctx_sb[:, hh * 128:(hh + 1) * 128], ctx_ps[:, hh * 128:(hh + 1) * 128],
                AF.Copy, scale=rec[:, hh:hh + 1]), deps=all_pv)
        prev_ctxev = t_cev
        t_ctxT = []
        for hc in range(8):
            ab = hc % 2
            t_tr = em.op("pe", lambda e, hc=hc, ab=ab: e.transpose(
                at_ps[ab][:, (hc % 8) * 128:(hc % 8 + 1) * 128],
                ctx_sb[:, hc * 128:(hc + 1) * 128], ids[:]),
                deps=[prev_ctxev] + ([prev_at[ab]] if prev_at[ab] else []) + ([prev_ctxT] if prev_ctxT else []))
            t_ev = em.op("dve", lambda e, hc=hc, ab=ab: e.tensor_copy(
                ctxT_sb[:, hc * 128:(hc + 1) * 128],
                at_ps[ab][:, (hc % 8) * 128:(hc % 8 + 1) * 128]), deps=[t_tr])
            prev_at[ab] = t_ev
            t_ctxT.append(t_ev)
        prev_ctxT = t_ctxT[-1]
        for sl in range(4):
            pb = sl % 2
            t_mm = em.ops_seq("pe", [
                (lambda e, hc=hc, sl=sl, pb=pb: e.matmul(
                    po_ps[pb][:], ctxT_sb[:, hc * 128:(hc + 1) * 128],
                    wo[:, hc * D + sl * 512:hc * D + (sl + 1) * 512],
                    start=(hc == 0), stop=(hc == 7)))
                for hc in range(8)
            ], deps=t_ctxT + [t_wo] + ([prev_po[pb]] if prev_po[pb] else []))
            sb = part_sb[:, pb * 512:(pb + 1) * 512]
            t_ev = em.op("dve", lambda e, sb=sb, pb=pb: e.tensor_copy(sb, po_ps[pb][:]),
                         deps=[t_mm] + ([prev_pstg[pb]] if prev_pstg[pb] else []))
            prev_po[pb] = t_ev
            t_st = em.dma("sp", lambda e, sb=sb, j=j, sl=sl: e.dma_start(
                out=p_out[j * 128:(j + 1) * 128, sl * 512:(sl + 1) * 512], in_=sb), deps=[t_ev])
            prev_pstg[pb] = t_st
            attn_stores.append(t_st)
        prev_q[qb] = Tok(em.sems["pe"].__hash__() and em.sems["pe"], em.counts["pe"]) if False else em.tok("pe")
    em.op("sp", lambda e: None, deps=attn_stores)
    em.finalize()
    return nc


MOE_I = 1408
IH = [(0, 6), (6, 11)]


def build_moe():
    nc = bass.Bass("TRN2", target_bir_lowering=False)
    x_in = nc.dram_tensor("x", [TOK, D], F32, kind="ExternalInput").ap()
    gwt_in = nc.dram_tensor("gwt", [D, 8], BF16, kind="ExternalInput").ap()
    wg_in = nc.dram_tensor("wg", [10, D, MOE_I], BF16, kind="ExternalInput").ap()
    wu_in = nc.dram_tensor("wu", [10, D, MOE_I], BF16, kind="ExternalInput").ap()
    wd_in = nc.dram_tensor("wd", [10, MOE_I, D], BF16, kind="ExternalInput").ap()
    id_in = nc.dram_tensor("ident", [128, 128], BF16, kind="ExternalInput").ap()
    xo = nc.dram_tensor("xo", [TOK, D], F32, kind="ExternalOutput").ap()

    em = Emitter(nc)
    HT = TOK // 2
    NTH = HT // 128
    xs = em.sbuf("mx", [128, NTH * D], F32)
    ids = em.sbuf("ids", [128, 128], BF16)
    gwt = em.sbuf("gwt", [128, KD * 8], BF16)
    slot = [em.sbuf(f"wslot{j}", [128, 16 * 768], BF16) for j in range(2)]
    g_act = em.sbuf("g_act", [128, 6 * HT], BF16)
    act = em.sbuf("act", [128, 6 * HT], BF16)
    cmb = em.sbuf("cmb", [128, NTH * 8], F32)
    nm = em.sbuf("nm", [128, 1], F32)
    probs = em.sbuf("probs", [128, 8], F32)
    prsum = em.sbuf("prsum", [128, 1], F32)
    m1 = em.sbuf("m1", [128, 1], F32)
    eq1 = em.sbuf("eq1", [128, 8], F32)
    maskd = em.sbuf("maskd", [128, 8], F32)
    m2 = em.sbuf("m2", [128, 1], F32)
    keep = em.sbuf("keep", [128, 8], F32)
    gw = em.sbuf("gw", [128, 8], F32)
    gsum = em.sbuf("gsum", [128, 1], F32)
    grecip = em.sbuf("grecip", [128, 1], F32)

    trp = [em.psum(f"trp{j}", [128, 1024], BF16) for j in range(2)]
    gu_ps = [em.psum(f"gu_ps{j}", [128, 512], F32) for j in range(2)]
    out_ps = [em.psum(f"out_ps{j}", [128, 512], F32) for j in range(2)]
    sc_ps = em.psum("sc_ps", [128, 512], F32)

    t_ids = em.dma("sp", lambda e: e.dma_start(out=ids[:], in_=id_in))
    t_gwt = em.dma("sp", lambda e: e.dma_start(
        out=gwt[:].rearrange("p (k o) -> p k o", k=KD), in_=gwt_in.rearrange("(k p) o -> p k o", p=128)))

    def load_slot(sl, src_ap, nchunk, width, deps):
        return em.dma("sp", lambda e: e.dma_start(
            out=slot[sl][:, 0:nchunk * width].rearrange("p (k w) -> p k w", w=width),
            in_=src_ap.rearrange("(k p) w -> p k w", p=128)), deps=deps)

    final_stores = []
    lnf = em.sbuf("lnf", [128, KD * (TOK // 2)], BF16)
    slot_last_use = {0: None, 1: None}
    for half in range(2):
        x_half = x_in[half * HT:(half + 1) * HT]
        lg = em.tok("pe") if half else None
        stats, ln_toks = emit_ln(em, x_half, lnf, ids, t_ids, ntiles=NTH, tr_ps=trp, guard=lg)
        x_toks = []
        for i in range(NTH):
            x_toks.append(em.dma("sp", lambda e, i=i, half=half: e.dma_start(
                out=xs[:, i * D:(i + 1) * D], in_=x_in[(half * NTH + i) * 128:(half * NTH + i + 1) * 128, :]),
                deps=final_stores if half else []))
        prev_r = None
        cmb_toks = []
        for i in range(NTH):
            t_mm = em.ops_seq("pe", [
                (lambda e, k=k, i=i: e.matmul(sc_ps[:, 0:8],
                                              lnf[:, (k * NTH + i) * 128:(k * NTH + i + 1) * 128],
                                              gwt[:, k * 8:(k + 1) * 8],
                                              start=(k == 0), stop=(k == KD - 1)))
                for k in range(KD)
            ], deps=ln_toks + [t_gwt] + ([prev_r] if prev_r else []))
            t_nm = em.op("dve", lambda e: e.reduce_max(nm[:], sc_ps[:, 0:8], axis=X_AX, negate=True),
                         deps=[t_mm] + ([prev_r] if prev_r else []))
            t_e = em.op("act", lambda e: e.activation(probs[:], sc_ps[:, 0:8], AF.Exp, bias=nm[:],
                                                      accum_out=prsum[:]), deps=[t_nm])
            t_m1 = em.op("dve", lambda e: e.reduce_max(m1[:], probs[:], axis=X_AX), deps=[t_e])
            t_eq = em.op("dve", lambda e: e.tensor_scalar(eq1[:], probs[:], m1[:], None, ALU.is_ge), deps=[t_m1])
            t_mk = em.op("dve", lambda e: e.scalar_tensor_tensor(maskd[:], eq1[:], -1e30, probs[:],
                                                                 ALU.mult, ALU.add), deps=[t_eq])
            t_m2 = em.op("dve", lambda e: e.reduce_max(m2[:], maskd[:], axis=X_AX), deps=[t_mk])
            t_kp = em.op("dve", lambda e: e.tensor_scalar(keep[:], probs[:], m2[:], None, ALU.is_ge), deps=[t_m2])
            t_gw = em.op("dve", lambda e: e.tensor_mul(gw[:], probs[:], keep[:]), deps=[t_kp])
            t_gs = em.op("dve", lambda e: e.reduce_sum(gsum[:], gw[:], axis=X_AX), deps=[t_gw])
            t_ge = em.op("dve", lambda e: e.scalar_tensor_tensor(gsum[:], prsum[:], 1e-20, gsum[:],
                                                                 ALU.mult, ALU.add), deps=[t_gs])
            t_gr = em.op("dve", lambda e: e.reciprocal(grecip[:], gsum[:]), deps=[t_ge])
            t_gr2 = em.op("dve", lambda e: e.tensor_scalar_mul(grecip[:], grecip[:], 0.5), deps=[t_gr])
            csl = cmb[:, i * 8:(i + 1) * 8]
            t_c = em.op("dve", lambda e, csl=csl: e.tensor_scalar(csl, gw[:], grecip[:], None, ALU.mult),
                        deps=[t_gr2])
            prev_r = t_c
            cmb_toks.append(t_c)

        schedule = []
        for e10 in range(10):
            for ihx in range(2):
                schedule.append(("g", e10, ihx))
                schedule.append(("u", e10, ihx))
                schedule.append(("d", e10, ihx))
        n_ph = len(schedule)

        def emit_load(idx):
            kind, e10, ihx = schedule[idx]
            c0, c1 = IH[ihx]
            nchunk = c1 - c0
            s = idx % 2
            dep = [slot_last_use[s]] if slot_last_use[s] else []
            if kind == "g":
                src = wg_in[e10][:, c0 * 128:c1 * 128]
                return em.dma("sp", lambda e, src=src, s=s, nchunk=nchunk: e.dma_start(
                    out=slot[s][:, 0:KD * nchunk * 128].rearrange("p (k w) -> p k w", k=KD),
                    in_=src.rearrange("(k p) w -> p k w", p=128)), deps=dep)
            if kind == "u":
                src = wu_in[e10][:, c0 * 128:c1 * 128]
                return em.dma("sp", lambda e, src=src, s=s, nchunk=nchunk: e.dma_start(
                    out=slot[s][:, 0:KD * nchunk * 128].rearrange("p (k w) -> p k w", k=KD),
                    in_=src.rearrange("(k p) w -> p k w", p=128)), deps=dep)
            src = wd_in[e10][c0 * 128:c1 * 128, :]
            return em.dma("sp", lambda e, src=src, s=s, nchunk=nchunk: e.dma_start(
                out=slot[s][:, 0:nchunk * D].rearrange("p (k w) -> p k w", w=D),
                in_=src.rearrange("(k p) w -> p k w", p=128)), deps=dep)

        load_toks = [emit_load(0), emit_load(1)]
        gu_guard = {0: None, 1: None}
        out_guard = {0: None, 1: None}
        g_done = {}
        act_done = {}
        xev_last = {i: None for i in range(NTH)}
        for idx in range(n_ph):
            kind, e10, ihx = schedule[idx]
            c0, c1 = IH[ihx]
            nchunk = c1 - c0
            s = idx % 2
            t_l = load_toks[idx]
            phase_compute = []
            if kind in ("g", "u"):
                slot_v = slot[s][:, 0:KD * nchunk * 128].rearrange("p (k w) -> p k w", k=KD)
                for ic in range(nchunk):
                    for ts in range(2):
                        pb = (ic * 2 + ts) % 2
                        t_mm = em.ops_seq("pe", [
                            (lambda e, k=k, ic=ic, ts=ts, pb=pb, slot_v=slot_v: e.matmul(
                                gu_ps[pb][:], slot_v[:, k, ic * 128:(ic + 1) * 128],
                                lnf[:, (k * NTH + ts * 4) * 128:(k * NTH + (ts + 1) * 4) * 128],
                                start=(k == 0), stop=(k == KD - 1)))
                            for k in range(KD)
                        ], deps=ln_toks + [t_l] + ([gu_guard[pb]] if gu_guard[pb] else []))
                        if kind == "g":
                            dst = g_act[:, ic * HT + ts * 512: ic * HT + (ts + 1) * 512]
                            t_ev = em.op("act", lambda e, dst=dst, pb=pb: e.activation(
                                dst, gu_ps[pb][:], AF.Silu), deps=[t_mm])
                        else:
                            asl = act[:, ic * HT + ts * 512: ic * HT + (ts + 1) * 512]
                            gsl = g_act[:, ic * HT + ts * 512: ic * HT + (ts + 1) * 512]
                            t_ev = em.op("dve", lambda e, asl=asl, gsl=gsl, pb=pb: e.tensor_tensor(
                                asl, gsl, gu_ps[pb][:], ALU.mult), deps=[t_mm, g_done[(e10, ihx)]])
                        gu_guard[pb] = t_ev
                        phase_compute.append(t_ev)
                if kind == "g":
                    g_done[(e10, ihx)] = phase_compute[-1]
                else:
                    act_done[(e10, ihx)] = phase_compute[-1]
            else:
                slot_v = slot[s][:, 0:nchunk * D].rearrange("p (k w) -> p k w", w=D)
                t_act = act_done[(e10, ihx)]
                for i_t in range(NTH):
                    for sl4 in range(4):
                        pb = (i_t * 4 + sl4) % 2
                        t_mm = em.ops_seq("pe", [
                            (lambda e, ic=ic, i_t=i_t, sl4=sl4, pb=pb, slot_v=slot_v: e.matmul(
                                out_ps[pb][:], act[:, ic * HT + i_t * 128: ic * HT + (i_t + 1) * 128],
                                slot_v[:, ic, sl4 * 512:(sl4 + 1) * 512],
                                start=(ic == 0), stop=(ic == nchunk - 1)))
                            for ic in range(nchunk)
                        ], deps=[t_l, t_act] + ([out_guard[pb]] if out_guard[pb] else []))
                        xt = xs[:, i_t * D + sl4 * 512: i_t * D + (sl4 + 1) * 512]
                        xdep = [t_mm, x_toks[i_t], cmb_toks[i_t]]
                        if xev_last[i_t]:
                            xdep.append(xev_last[i_t])
                        if e10 < 8:
                            csl = cmb[:, i_t * 8 + e10: i_t * 8 + e10 + 1]
                            t_ev = em.op("dve", lambda e, xt=xt, pb=pb, csl=csl: e.scalar_tensor_tensor(
                                xt, out_ps[pb][:], csl, xt, ALU.mult, ALU.add), deps=xdep)
                        else:
                            t_ev = em.op("dve", lambda e, xt=xt, pb=pb: e.scalar_tensor_tensor(
                                xt, out_ps[pb][:], 0.5, xt, ALU.mult, ALU.add), deps=xdep)
                        out_guard[pb] = t_ev
                        xev_last[i_t] = t_ev
                        phase_compute.append(t_ev)
            slot_last_use[s] = em.tok("pe")
            if idx + 2 < n_ph:
                load_toks.append(emit_load(idx + 2))
            elif half == 0 and idx + 2 >= n_ph:
                load_toks.append(None)
        final_stores = []
        for i in range(NTH):
            t_st = em.dma("sp", lambda e, i=i, half=half: e.dma_start(
                out=xo[(half * NTH + i) * 128:(half * NTH + i + 1) * 128, :],
                in_=xs[:, i * D:(i + 1) * D]), deps=[xev_last[i]])
            final_stores.append(t_st)
    em.op("sp", lambda e: None, deps=final_stores)
    em.finalize()

    return nc



def build_inproj():
    nc = bass.Bass("TRN2", target_bir_lowering=False)
    x_in = nc.dram_tensor("x", [TOK, D], F32, kind="ExternalInput").ap()
    ipt_in = nc.dram_tensor("ipt", [D, 8384], BF16, kind="ExternalInput").ap()
    id_in = nc.dram_tensor("ident", [128, 128], BF16, kind="ExternalInput").ap()
    xb_out = nc.dram_tensor("xb", [4096, TOK], BF16, kind="ExternalOutput").ap()
    z_out = nc.dram_tensor("z", [4096, TOK], BF16, kind="ExternalOutput").ap()
    bc_out = nc.dram_tensor("bc", [128, TOK], BF16, kind="ExternalOutput").ap()
    dt_out = nc.dram_tensor("dt", [64, TOK], F32, kind="ExternalOutput").ap()

    em = Emitter(nc)
    lnf = em.sbuf("lnf", [128, KD * TOK], BF16)
    ids = em.sbuf("ids", [128, 128], BF16)
    wbuf = [em.sbuf(f"ipw{j}", [128, KD * 128], BF16) for j in range(2)]
    stage = em.sbuf("stage", [128, 2 * 512], BF16)
    stage_f = em.sbuf("stage_f", [128, 512], F32)
    trp = [em.psum(f"trp{j}", [128, 1024], BF16) for j in range(2)]
    pp = [em.psum(f"pp{j}", [128, 512], F32) for j in range(2)]

    t_ids = em.dma("sp", lambda e: e.dma_start(out=ids[:], in_=id_in))
    stats, ln_toks = emit_ln(em, x_in, lnf, ids, t_ids, tr_ps=trp)

    wu = {0: None, 1: None}
    wload = {}

    def load_w(oc, width=128):
        s = oc % 2
        dep = [wu[s]] if wu[s] else []
        tk = em.dma("sp", lambda e, oc=oc, s=s, width=width: e.dma_start(
            out=wbuf[s][:, 0:KD * width].rearrange("p (k w) -> p k w", k=KD),
            in_=ipt_in[:, oc * 128:oc * 128 + width].rearrange("(k p) w -> p k w", p=128)), deps=dep)
        wload[oc] = tk

    load_w(0)
    load_w(1)
    prev_pp = {0: None, 1: None}
    prev_stg = {0: None, 1: None}
    last_st = []
    for oc in range(66):
        s = oc % 2
        width = 64 if oc == 65 else 128
        t_l = wload[oc]
        for ts in range(TOK // 512):
            pb = ts % 2
            t_mm = em.ops_seq("pe", [
                (lambda e, k=k, oc=oc, ts=ts, pb=pb, s=s, width=width: e.matmul(
                    pp[pb][:, 0:width * 4] if False else pp[pb][0:width, 0:512],
                    wbuf[s][:, k * width:(k + 1) * width] if False else
                    wbuf[s][:, 0:KD * width].rearrange("p (k w) -> p k w", k=KD)[:, k, :],
                    lnf[:, (k * NT + ts * 4) * 128:(k * NT + (ts + 1) * 4) * 128],
                    start=(k == 0), stop=(k == KD - 1)))
                for k in range(KD)
            ], deps=ln_toks + [t_l] + ([prev_pp[pb]] if prev_pp[pb] else []))
            if oc == 65:
                sb = stage_f[0:64, :]
                t_ev = em.op("dve", lambda e, sb=sb, pb=pb: e.tensor_copy(sb, pp[pb][0:64, 0:512]),
                             deps=[t_mm] + ([prev_stg[2] if 2 in prev_stg else None] if False else []))
                t_st = em.dma("sp", lambda e, sb=sb, ts=ts: e.dma_start(
                    out=dt_out[:, ts * 512:(ts + 1) * 512], in_=sb), deps=[t_ev])
            else:
                sb = stage[:, (ts % 2) * 512:(ts % 2 + 1) * 512]
                t_ev = em.op("dve", lambda e, sb=sb, pb=pb: e.tensor_copy(sb, pp[pb][:]),
                             deps=[t_mm] + ([prev_stg[ts % 2]] if prev_stg[ts % 2] else []))
                if oc < 32:
                    dst = xb_out[oc * 128:(oc + 1) * 128, ts * 512:(ts + 1) * 512]
                elif oc < 64:
                    dst = z_out[(oc - 32) * 128:(oc - 31) * 128, ts * 512:(ts + 1) * 512]
                else:
                    dst = bc_out[:, ts * 512:(ts + 1) * 512]
                t_st = em.dma("sp", lambda e, sb=sb, dst=dst: e.dma_start(out=dst, in_=sb), deps=[t_ev])
            prev_pp[pb] = t_ev
            prev_stg[ts % 2] = t_st
            last_st.append(t_st)
        wu[s] = em.tok("pe")
        if oc + 2 < 66:
            load_w(oc + 2, 64 if oc + 2 == 65 else 128)
    em.op("sp", lambda e: None, deps=last_st)
    em.finalize()
    return nc



def build_outproj():
    nc = bass.Bass("TRN2", target_bir_lowering=False)
    x_in = nc.dram_tensor("x", [TOK, D], F32, kind="ExternalInput").ap()
    y_in = nc.dram_tensor("yt", [NT, 32, 128, 128], BF16, kind="ExternalInput").ap()
    opt_in = nc.dram_tensor("opt", [4096, D], BF16, kind="ExternalInput").ap()
    xo = nc.dram_tensor("xo", [TOK, D], F32, kind="ExternalOutput").ap()

    em = Emitter(nc)
    opt = em.sbuf("opt", [128, 32 * D], BF16)
    yt = em.sbuf("yt", [128, 2 * 4096], BF16)
    xs = em.sbuf("xs", [128, 2 * D], F32)
    pp = [em.psum(f"pp{j}", [128, 512], F32) for j in range(2)]

    t_w = []
    for k4 in range(4):
        t_w.append(em.dma("sp", lambda e, k4=k4: e.dma_start(
            out=opt[:, k4 * 8 * D:(k4 + 1) * 8 * D].rearrange("p (k w) -> p k w", k=8),
            in_=opt_in[k4 * 1024:(k4 + 1) * 1024, :].rearrange("(k p) w -> p k w", p=128))))
    prev_y = {0: None, 1: None}
    prev_x = {0: None, 1: None}
    prev_pp = {0: None, 1: None}
    stores = []
    for i in range(NT):
        lb = i % 2
        t_y = em.dma("sp", lambda e, i=i, lb=lb: e.dma_start(
            out=yt[:, lb * 4096:(lb + 1) * 4096].rearrange("p (c w) -> p c w", c=32),
            in_=y_in[i].rearrange("c p w -> p c w")), deps=[prev_y[lb]] if prev_y[lb] else [])
        t_x = em.dma("sp", lambda e, i=i, lb=lb: e.dma_start(
            out=xs[:, lb * D:(lb + 1) * D], in_=x_in[i * 128:(i + 1) * 128, :]),
            deps=[prev_x[lb]] if prev_x[lb] else [])
        tile_out = []
        for sl in range(4):
            pb = sl % 2
            t_mm = em.ops_seq("pe", [
                (lambda e, k=k, i=i, sl=sl, pb=pb, lb=lb: e.matmul(
                    pp[pb][:], yt[:, lb * 4096 + k * 128:lb * 4096 + (k + 1) * 128],
                    opt[:, k * D + sl * 512:k * D + (sl + 1) * 512],
                    start=(k == 0), stop=(k == 31)))
                for k in range(32)
            ], deps=t_w + [t_y] + ([prev_pp[pb]] if prev_pp[pb] else []))
            xt = xs[:, lb * D + sl * 512: lb * D + (sl + 1) * 512]
            t_ev = em.op("dve", lambda e, xt=xt, pb=pb: e.scalar_tensor_tensor(
                xt, pp[pb][:], 1.0, xt, ALU.mult, ALU.add), deps=[t_mm, t_x])
            prev_pp[pb] = t_ev
            tile_out.append(t_ev)
        t_st = em.dma("sp", lambda e, i=i, lb=lb: e.dma_start(
            out=xo[i * 128:(i + 1) * 128, :], in_=xs[:, lb * D:(lb + 1) * D]), deps=tile_out)
        prev_x[lb] = t_st
        prev_y[lb] = em.tok("pe")
        stores.append(t_st)
    em.op("sp", lambda e: None, deps=stores)
    em.finalize()
    return nc


NCH = 16
NCP = 32
HD = 64


def build_ssd(nhp=NCH, stage=99, ncpl=NCP, parts=15, dbg=False):
    nc = bass.Bass("TRN2", target_bir_lowering=False)
    xb_in = nc.dram_tensor("xb", [2048, T], BF16, kind="ExternalInput").ap()
    z_in = nc.dram_tensor("z", [2048, T], BF16, kind="ExternalInput").ap()
    bc_in = nc.dram_tensor("bc", [128, T], BF16, kind="ExternalInput").ap()
    dt_in = nc.dram_tensor("dt", [32, T], F32, kind="ExternalInput").ap()
    alog_in = nc.dram_tensor("alog", [32, 1], F32, kind="ExternalInput").ap()
    dtb_in = nc.dram_tensor("dtbias", [32, 1], F32, kind="ExternalInput").ap()
    cw_in = nc.dram_tensor("convw", [128, 64], F32, kind="ExternalInput").ap()
    cb_in = nc.dram_tensor("convb", [128, 16], F32, kind="ExternalInput").ap()
    ds_in = nc.dram_tensor("dskip", [128, 16], F32, kind="ExternalInput").ap()
    mk_in = nc.dram_tensor("maskd", [128, 64], F32, kind="ExternalInput").ap()
    id_in = nc.dram_tensor("ident", [128, 128], BF16, kind="ExternalInput").ap()
    idf_in = nc.dram_tensor("identf", [128, 128], F32, kind="ExternalInput").ap()
    y_out = nc.dram_tensor("y", [2048, T], BF16, kind="ExternalOutput").ap()
    ikind = "ExternalOutput" if dbg else "Internal"
    acum_d = nc.dram_tensor("acum_d", [32, T], F32, kind=ikind).ap()
    brd_d = nc.dram_tensor("brd_d", [3, 32, T], BF16, kind=ikind).ap()
    lam_d = nc.dram_tensor("lam_d", [32, 64], F32, kind=ikind).ap()

    em = Emitter(nc)
    ids = em.sbuf("ids", [128, 128], BF16)
    mask = em.sbuf("maskd", [128, 64], F32)
    bcs = em.sbuf("bcs", [128, T], BF16)
    c2 = em.sbuf("c2", [128, T], BF16)
    btok = em.sbuf("btok", [128, NCP * 64], BF16)
    s2 = em.sbuf("s2", [128, NCP * 64], BF16)
    aw1 = em.sbuf("aw1", [32, T], F32)
    aw2 = em.sbuf("aw2", [32, T], F32)
    stage_bf = em.sbuf("stage_bf", [32, T], BF16)
    lam = em.sbuf("lam", [32, 64], F32)
    cwv = em.sbuf("cwv", [128, 16 * 4], F32)
    cbv = em.sbuf("cbv", [128, 16], F32)
    dsv = em.sbuf("dsv", [128, 16], F32)
    alog = em.sbuf("alog", [32, 1], F32)
    dtbias = em.sbuf("dtbias", [32, 1], F32)
    negea = em.sbuf("negea", [32, 1], F32)
    brd = em.sbuf("brd", [128, 3 * T], BF16)
    lamb = em.sbuf("lamb", [128, 64], F32)
    xpad = em.sbuf("xpad", [128, 4 + T], BF16)
    cacc = em.sbuf("cacc", [128, T], BF16)
    xact = em.sbuf("xact", [128, T], BF16)
    xdisc = em.sbuf("xdisc", [128, T], BF16)
    xdd = em.sbuf("xdd", [128, T], BF16)
    xd_tok = em.sbuf("xd_tok", [128, T], BF16)
    xdd_tok = em.sbuf("xdd_tok", [128, T], BF16)
    cd2 = em.sbuf("cd2", [128, T], BF16)
    dbuf1 = em.sbuf("dbuf1", [128, 4096], F32)
    acumT = em.sbuf("acumT", [128, 1024], F32)
    idsf = em.sbuf("idsf", [128, 128], F32)
    ltb = em.sbuf("ltb", [128, 4096], BF16)
    mtb = em.sbuf("mtb", [128, 4096], BF16)
    ytok = xdisc
    yfm = xdd
    zc = cacc
    rf = em.sbuf("rf", [128, 64], F32)
    rb = em.sbuf("rb", [128, 64], BF16)

    tr_ps = [em.psum(f"tr_ps{j}", [128, 1024], BF16) for j in range(2)]
    y_ps = [em.psum(f"y_ps{j}", [128, 512], F32) for j in range(2)]
    st_ps = [em.psum(f"st_ps{j}", [128, 512], F32) for j in range(2)]
    m_ps = em.psum("m_ps", [128, 512], F32)
    ym_ps = em.psum("ym_ps", [128, 512], F32)

    t_ids = em.dma("sp", lambda e: e.dma_start(out=ids[:], in_=id_in))
    t_idf = em.dma("sp", lambda e: e.dma_start(out=idsf[:], in_=idf_in))
    t_mk = em.dma("sp", lambda e: e.dma_start(out=mask[:], in_=mk_in))
    t_bc = em.dma("sp", lambda e: e.dma_start(out=bcs[:], in_=bc_in))
    t_cw = em.dma("sp", lambda e: e.dma_start(out=cwv[:], in_=cw_in))
    t_cb = em.dma("sp", lambda e: e.dma_start(out=cbv[:], in_=cb_in))
    t_ds = em.dma("sp", lambda e: e.dma_start(out=dsv[:], in_=ds_in))
    t_al = em.dma("sp", lambda e: e.dma_start(out=alog[:], in_=alog_in))
    t_dtb = em.dma("sp", lambda e: e.dma_start(out=dtbias[:], in_=dtb_in))

    t_dt = em.dma("sp", lambda e: e.dma_start(out=aw1[:], in_=dt_in))
    t_v = em.op("dve", lambda e: e.tensor_scalar(aw1[:], aw1[:], dtbias[:, 0:1], None, ALU.add),
                deps=[t_dt, t_dtb])
    t_ab = em.op("act", lambda e: e.activation(aw2[:], aw1[:], AF.Abs), deps=[t_v])
    t_ex = em.op("act", lambda e: e.activation(aw2[:], aw2[:], AF.Exp, scale=-1.0), deps=[t_ab])
    t_ln1 = em.op("act", lambda e: e.activation(aw2[:], aw2[:], AF.Ln, bias=1.0), deps=[t_ex])
    t_rl = em.op("dve", lambda e: e.tensor_scalar_max(aw1[:], aw1[:], 0.0), deps=[t_v, t_ln1])
    t_dtf = em.op("dve", lambda e: e.tensor_add(aw1[:], aw1[:], aw2[:]), deps=[t_rl])
    t_sbf = em.op("dve", lambda e: e.tensor_copy(stage_bf[:], aw1[:]), deps=[t_dtf])
    t_w0 = em.dma("sp", lambda e: e.dma_start(out=brd_d[0], in_=stage_bf[:]), deps=[t_sbf])
    t_ea = em.op("act", lambda e: e.activation(negea[:], alog[:], AF.Exp), deps=[t_al])
    t_ne = em.op("dve", lambda e: e.tensor_scalar_mul(negea[:], negea[:], -1.0), deps=[t_ea])
    t_ad = em.op("dve", lambda e: e.tensor_scalar(aw2[:], aw1[:], negea[:, 0:1], None, ALU.mult),
                 deps=[t_dtf, t_ne])
    t_cum = em.ops_seq("dve", [
        (lambda e, c=c: e.tensor_tensor_scan(aw1[:, c * 64:(c + 1) * 64], aw2[:, c * 64:(c + 1) * 64],
                                             aw2[:, c * 64:(c + 1) * 64], 0.0, ALU.add, ALU.bypass))
        for c in range(64)
    ], deps=[t_ad, t_w0, t_sbf])
    t_wa = em.dma("sp", lambda e: e.dma_start(out=acum_d, in_=aw1[:]), deps=[t_cum])
    acT_toks = []
    prev_aT = {0: None, 1: None}
    for it in range(NCP):
        pbk = it % 2
        t_tr = em.op("pe", lambda e, it=it, pbk=pbk: e.transpose(
            m_ps[0:128, pbk * 64:pbk * 64 + 32].bitcast(F32) if False else
            st_ps[pbk][0:128, 0:32],
            aw1[0:32, it * 128:(it + 1) * 128], idsf[0:32, 0:32]),
            deps=[t_cum, t_idf] + ([prev_aT[pbk]] if prev_aT[pbk] else []))
        t_ev = em.op("dve", lambda e, it=it, pbk=pbk: e.tensor_copy(
            acumT[:].rearrange("p (h cp) -> p h cp", cp=NCP)[:, :, it],
            st_ps[pbk][0:128, 0:32]), deps=[t_tr])
        prev_aT[pbk] = t_ev
        acT_toks.append(t_ev)
    t_lam = em.op("act", lambda e: e.activation(
        lam[:], aw1[:].rearrange("p (c j) -> p c j", j=64)[:, :, 63], AF.Exp), deps=[t_cum])
    t_wl = em.dma("sp", lambda e: e.dma_start(out=lam_d, in_=lam[:]), deps=[t_lam])
    aend_bc = aw1[:].rearrange("p (c j) -> p c j", j=64)[:, :, 63:64].broadcast_to([32, 64, 64])
    t_de = em.op("dve", lambda e: e.tensor_tensor(
        aw2[:].rearrange("p (c j) -> p c j", j=64), aend_bc,
        aw1[:].rearrange("p (c j) -> p c j", j=64), ALU.subtract), deps=[t_cum, t_wa])
    t_dex = em.op("act", lambda e: e.activation(stage_bf[:], aw2[:], AF.Exp), deps=[t_de, t_w0])
    t_w1 = em.dma("sp", lambda e: e.dma_start(out=brd_d[1], in_=stage_bf[:]), deps=[t_dex])
    t_eac = em.op("act", lambda e: e.activation(stage_bf[:], aw1[:], AF.Exp), deps=[t_w1, t_cum])
    t_w2 = em.dma("sp", lambda e: e.dma_start(out=brd_d[2], in_=stage_bf[:]), deps=[t_eac])
    t_c2a = em.op("dve", lambda e: e.tensor_copy(c2[0:64, :], bcs[64:128, :]), deps=[t_bc])
    t_c2b = em.op("dve", lambda e: e.tensor_copy(c2[64:128, :], bcs[64:128, :]), deps=[t_bc])
    bt_toks = []
    prev_bt = {0: None, 1: None}
    for it in range(NCP):
        pbk = it % 2
        t_tr = em.op("pe", lambda e, it=it, pbk=pbk: e.transpose(
            tr_ps[pbk][0:128, 0:64], bcs[0:64, it * 128:(it + 1) * 128], ids[0:64, 0:64]),
            deps=[t_bc, t_ids] + ([prev_bt[pbk]] if prev_bt[pbk] else []))
        t_ev = em.op("dve", lambda e, it=it, pbk=pbk: e.tensor_copy(
            btok[:, it * 64:(it + 1) * 64], tr_ps[pbk][0:128, 0:64]), deps=[t_tr])
        prev_bt[pbk] = t_ev
        bt_toks.append(t_ev)
    s2_toks = []
    prev_s2 = None
    for cp in range(NCP):
        t_mm = em.ops_seq("pe", [
            (lambda e, cp=cp, ch=ch: e.matmul(
                m_ps[ch * 64:(ch + 1) * 64, 0:64],
                bcs[0:64, (2 * cp + ch) * 64:(2 * cp + ch + 1) * 64],
                c2[0:64, (2 * cp + ch) * 64:(2 * cp + ch + 1) * 64],
                start=True, stop=True))
            for ch in range(2)
        ], deps=[t_bc, t_c2a] + ([prev_s2] if prev_s2 else []))
        t_ev = em.op("dve", lambda e, cp=cp: e.tensor_copy(s2[:, cp * 64:(cp + 1) * 64], m_ps[:, 0:64]),
                     deps=[t_mm])
        prev_s2 = t_ev
        s2_toks.append(t_ev)

    y_stores = []
    prev_hp = {}
    if len(acT_toks) >= 2:
        prev_hp[("stps", 0)] = acT_toks[-2]
        prev_hp[("stps", 1)] = acT_toks[-1]
    for hp in range(nhp):
        brd_deps = [t_w0, t_w1, t_w2] + ([prev_hp.get("brd")] if prev_hp.get("brd") else [])
        t_brd = []
        for a in range(2):
            t_brd.append(em.dma("sp", lambda e, hp=hp, a=a: e.dma_start(
                out=brd[a * 64:(a + 1) * 64, :].rearrange("p (pl t) -> p pl t", pl=3),
                in_=brd_d[:, 2 * hp + a, :].unsqueeze(0).broadcast_to([64, 3, T])),
                deps=brd_deps))
        t_lamb = []
        for a in range(2):
            t_lamb.append(em.dma("sp", lambda e, hp=hp, a=a: e.dma_start(
                out=lamb[a * 64:(a + 1) * 64, :],
                in_=lam_d[2 * hp + a].unsqueeze(0).broadcast_to([64, 64])),
                deps=[t_wl] + ([prev_hp.get("lamb")] if prev_hp.get("lamb") else [])))
        t_xb = em.dma("sp", lambda e, hp=hp: e.dma_start(
            out=xpad[:, 4:4 + T], in_=xb_in[hp * 128:(hp + 1) * 128, :]),
            deps=[prev_hp.get("xpad")] if prev_hp.get("xpad") else [])
        t_pad = em.op("dve", lambda e: e.memset(xpad[:, 1:4], 0.0),
                      deps=[prev_hp.get("xpad")] if prev_hp.get("xpad") else [])
        w0c = cwv[:, hp * 4 + 0: hp * 4 + 1]
        t_cv = em.op("dve", lambda e, w0c=w0c: e.tensor_scalar(
            cacc[:], xpad[:, 1:1 + T], w0c, None, ALU.mult),
            deps=[t_xb, t_pad, t_cw] + ([prev_hp.get("cacc")] if prev_hp.get("cacc") else []))
        for tau in range(1, 4):
            wc = cwv[:, hp * 4 + tau: hp * 4 + tau + 1]
            t_cv = em.op("dve", lambda e, wc=wc, tau=tau: e.scalar_tensor_tensor(
                cacc[:], xpad[:, 1 + tau:1 + tau + T], wc, cacc[:], ALU.mult, ALU.add),
                deps=[t_cv])
        cbc = cbv[:, hp:hp + 1]
        t_sil = em.op("act", lambda e, cbc=cbc: e.activation(
            xact[:], cacc[:], AF.Silu, bias=cbc), deps=[t_cv, t_cb] +
            ([prev_hp.get("xact")] if prev_hp.get("xact") else []))
        if stage <= 1:
            continue
        t_xd = em.op("dve", lambda e: e.tensor_tensor(xdisc[:], xact[:], brd[:, 0:T], ALU.mult),
                     deps=[t_sil] + t_brd + ([prev_hp.get("xdisc")] if prev_hp.get("xdisc") else []))
        t_xdd = em.op("dve", lambda e: e.tensor_tensor(xdd[:], xdisc[:], brd[:, T:2 * T], ALU.mult),
                      deps=[t_xd] + t_brd + ([prev_hp.get("xdd")] if prev_hp.get("xdd") else []))
        t_cd = em.op("dve", lambda e: e.tensor_tensor(cd2[:], c2[:], brd[:, 2 * T:3 * T], ALU.mult),
                     deps=[t_c2a, t_c2b] + t_brd + ([prev_hp.get("cd2")] if prev_hp.get("cd2") else []))
        xtok_toks = []
        for it in range(NCP):
            pbk = it % 2
            g1 = prev_hp.get(("trb", pbk))
            t_t1 = em.op("pe", lambda e, it=it, pbk=pbk: e.transpose(
                tr_ps[pbk][:, 0:128], xdisc[:, it * 128:(it + 1) * 128], ids[:]),
                deps=[t_xd, t_ids] + ([g1] if g1 else []))
            t_e1 = em.op("dve", lambda e, it=it, pbk=pbk: e.tensor_copy(
                xd_tok[:, it * 128:(it + 1) * 128], tr_ps[pbk][:, 0:128]), deps=[t_t1] +
                ([prev_hp.get(("xtokr", it))] if prev_hp.get(("xtokr", it)) else []))
            t_t2 = em.op("pe", lambda e, it=it, pbk=pbk: e.transpose(
                tr_ps[pbk][:, 128:256], xdd[:, it * 128:(it + 1) * 128], ids[:]),
                deps=[t_xdd, t_ids, t_e1] + ([g1] if g1 else []))
            t_e2 = em.op("dve", lambda e, it=it, pbk=pbk: e.tensor_copy(
                xdd_tok[:, it * 128:(it + 1) * 128], tr_ps[pbk][:, 128:256]), deps=[t_t2])
            prev_hp[("trb", pbk)] = t_e2
            xtok_toks.append((t_e1, t_e2))
        if stage <= 2:
            continue
        d_deps = [t_wa] + ([prev_hp.get("dbufs")] if prev_hp.get("dbufs") else [])
        ac2 = acum_d[2 * hp:2 * hp + 2, :]
        t_d1 = []
        for cc in range(2):
            t_d1.append(em.dma("sp", lambda e, cc=cc, ac2=ac2: e.dma_start(
                out=dbuf1[cc * 64:(cc + 1) * 64, :],
                in_=ac2.rearrange("h (cp c i) -> c h cp i", cp=NCP, c=2)[cc]
                    .unsqueeze(0).broadcast_to([64, 2, NCP, 64])), deps=d_deps))
        t_dsub = em.op("dve", lambda e, hp=hp: e.tensor_tensor(
            dbuf1[:].rearrange("p (hcp i) -> p hcp i", i=64),
            dbuf1[:].rearrange("p (hcp i) -> p hcp i", i=64),
            acumT[:, hp * 64:(hp + 1) * 64].unsqueeze(2).broadcast_to([128, 64, 64]),
            ALU.subtract), deps=t_d1 + acT_toks[-1:])
        t_dmask = em.op("dve", lambda e: e.tensor_tensor(
            dbuf1[:].rearrange("p (hcp i) -> p hcp i", i=64), dbuf1[:].rearrange("p (hcp i) -> p hcp i", i=64),
            mask[:].unsqueeze(1).broadcast_to([128, NCP * 2, 64]), ALU.add), deps=[t_dsub, t_mk])
        t_lt = em.op("act", lambda e: e.activation(ltb[:], dbuf1[:], AF.Exp),
                     deps=[t_dmask] + ([prev_hp.get("ltb")] if prev_hp.get("ltb") else []))
        t_mt = em.op("dve", lambda e: e.tensor_tensor(
            mtb[:].rearrange("p (h cpi) -> p h cpi", h=2),
            ltb[:].rearrange("p (h cpi) -> p h cpi", h=2),
            s2[:].unsqueeze(1).broadcast_to([128, 2, NCP * 64]),
            ALU.mult), deps=[t_lt, s2_toks[-1]] + ([prev_hp.get("mtb")] if prev_hp.get("mtb") else []))
        if stage <= 3:
            continue
        t_r0 = em.op("dve", lambda e: e.memset(rf[:], 0.0),
                     deps=[prev_hp.get("rf")] if prev_hp.get("rf") else [])
        t_rb = em.op("dve", lambda e: e.tensor_copy(rb[:], rf[:]), deps=[t_r0])
        yev_toks = []
        for cp in range(ncpl):
            ypb = cp % 2
            y_guard = prev_hp.get(("yps", ypb))
            yev = None
            for chh in range(2):
                c = 2 * cp + chh
                mm_list = []
                for h in range(2):
                    if parts & 1:
                        mm_list.append(lambda e, cp=cp, chh=chh, h=h, ypb=ypb: e.matmul(
                            y_ps[ypb][chh * 64:(chh + 1) * 64, h * 64:(h + 1) * 64],
                            mtb[chh * 64:(chh + 1) * 64, h * 2048 + cp * 64: h * 2048 + (cp + 1) * 64],
                            xd_tok[chh * 64:(chh + 1) * 64, cp * 128 + h * 64: cp * 128 + (h + 1) * 64],
                            start=True, stop=True))
                    if parts & 2:
                        if h == chh:
                            out_ap = y_ps[ypb][chh * 64:(chh + 1) * 64, 128 + h * 64: 128 + (h + 1) * 64]
                        elif chh == 0:
                            out_ap = ym_ps[0:64, 0:64]
                        else:
                            out_ap = m_ps[64:128, 0:64]
                        mm_list.append(lambda e, c=c, chh=chh, h=h, out_ap=out_ap: e.matmul(
                            out_ap,
                            cd2[h * 64:(h + 1) * 64, c * 64:(c + 1) * 64],
                            rb[h * 64:(h + 1) * 64, :],
                            start=True, stop=True))
                cross_guard = [prev_hp.get("ycross")] if prev_hp.get("ycross") else [s2_toks[-1]]
                t_ymm = em.ops_seq("pe", mm_list,
                                   deps=[t_mt, xtok_toks[cp][0], t_cd, t_rb] + cross_guard +
                                        ([y_guard] if y_guard else []))
                spb = c % 2
                st_guard = prev_hp.get(("stps", spb))
                t_smm = em.ops_seq("pe", [] if not (parts & 4) else [
                    (lambda e, cp=cp, chh=chh, h=h, spb=spb: e.matmul(
                        st_ps[spb][h * 64:(h + 1) * 64, 0:64],
                        btok[chh * 64:(chh + 1) * 64, cp * 64:(cp + 1) * 64],
                        xdd_tok[chh * 64:(chh + 1) * 64, cp * 128 + h * 64: cp * 128 + (h + 1) * 64],
                        start=True, stop=True))
                    for h in range(2)
                ], deps=[bt_toks[cp], xtok_toks[cp][1], t_rb] + ([st_guard] if st_guard else []))
                lcol = lamb[:, c:c + 1]
                if parts & 8:
                    t_scan = em.op("dve", lambda e, lcol=lcol, spb=spb: e.scalar_tensor_tensor(
                        rf[:], rf[:], lcol, st_ps[spb][:, 0:64], ALU.mult, ALU.add),
                        deps=[t_smm, t_ymm] + t_lamb)
                else:
                    t_scan = em.op("dve", lambda e: None, deps=[t_smm, t_ymm])
                prev_hp[("stps", spb)] = t_scan
                t_rb = em.op("dve", lambda e: e.tensor_copy(rb[:], rf[:]), deps=[t_scan, t_ymm])
                yev = t_ymm
            ydeps = [yev, xtok_toks[cp][0], xtok_toks[cp][1]] + \
                    ([prev_hp.get(("ytokr", cp))] if prev_hp.get(("ytokr", cp)) else [])
            t_yc = em.op("act", lambda e, cp=cp, ypb=ypb: e.activation(
                ytok[:, cp * 128:(cp + 1) * 128], y_ps[ypb][:, 0:128], AF.Copy), deps=ydeps)
            t_yev = em.op("dve", lambda e, cp=cp, ypb=ypb: e.tensor_tensor(
                ytok[0:64, cp * 128:cp * 128 + 64], ytok[0:64, cp * 128:cp * 128 + 64],
                y_ps[ypb][0:64, 128:192], ALU.add), deps=[t_yc])
            t_yev = em.op("dve", lambda e, cp=cp, ypb=ypb: e.tensor_tensor(
                ytok[64:128, cp * 128 + 64:(cp + 1) * 128], ytok[64:128, cp * 128 + 64:(cp + 1) * 128],
                y_ps[ypb][64:128, 192:256], ALU.add), deps=[t_yev])
            t_ya1 = em.op("dve", lambda e, cp=cp: e.tensor_tensor(
                ytok[0:64, cp * 128 + 64:(cp + 1) * 128], ytok[0:64, cp * 128 + 64:(cp + 1) * 128],
                ym_ps[0:64, 0:64], ALU.add), deps=[t_yev])
            t_ya2 = em.op("dve", lambda e, cp=cp: e.tensor_tensor(
                ytok[64:128, cp * 128:cp * 128 + 64], ytok[64:128, cp * 128:cp * 128 + 64],
                m_ps[64:128, 0:64], ALU.add), deps=[t_ya1])
            prev_hp["ycross"] = t_ya2
            t_yev = t_ya2
            prev_hp[("yps", ypb)] = t_yev
            yev_toks.append(t_yev)
        if stage <= 4:
            continue
        yfm_toks = []
        for it in range(NCP):
            pbk = it % 2
            g1 = prev_hp.get(("trb", pbk))
            t_t = em.op("pe", lambda e, it=it, pbk=pbk: e.transpose(
                tr_ps[pbk][:, 0:128], ytok[:, it * 128:(it + 1) * 128], ids[:]),
                deps=[yev_toks[it % len(yev_toks)], t_ids] + ([g1] if g1 else []))
            t_e = em.op("dve", lambda e, it=it, pbk=pbk: e.tensor_copy(
                yfm[:, it * 128:(it + 1) * 128], tr_ps[pbk][:, 0:128]),
                deps=[t_t, xtok_toks[NCP - 1][1]] +
                ([prev_hp.get(("yfmr", it))] if prev_hp.get(("yfmr", it)) else []))
            prev_hp[("trb", pbk)] = t_e
            prev_hp[("ytokr", it)] = t_e
            yfm_toks.append(t_e)
        t_z = em.dma("sp", lambda e, hp=hp: e.dma_start(
            out=zc[:], in_=z_in[hp * 128:(hp + 1) * 128, :]),
            deps=[t_sil] + ([prev_hp.get("zc")] if prev_hp.get("zc") else []))
        t_sz = em.op("act", lambda e: e.activation(zc[:], zc[:], AF.Silu), deps=[t_z])
        dsc = dsv[:, hp:hp + 1]
        t_y1 = em.op("dve", lambda e, dsc=dsc: e.scalar_tensor_tensor(
            xdisc[:], xact[:], dsc, yfm[:], ALU.mult, ALU.add),
            deps=yfm_toks + [t_sil, t_ds] + [prev_hp.get("xdisc2")] if prev_hp.get("xdisc2") else
            yfm_toks + [t_sil, t_ds])
        t_y2 = em.op("dve", lambda e: e.tensor_mul(yfm[:], xdisc[:], zc[:]), deps=[t_y1, t_sz])
        t_yst = em.dma("sp", lambda e, hp=hp: e.dma_start(
            out=y_out[hp * 128:(hp + 1) * 128, :], in_=yfm[:]), deps=[t_y2])
        y_stores.append(t_yst)
        prev_hp["brd"] = Tok(t_cd.sem, t_cd.cnt)
        prev_hp["lamb"] = em.tok("dve")
        prev_hp["xpad"] = t_y2
        prev_hp["cacc"] = t_y2
        prev_hp["xact"] = t_y1
        prev_hp["xdisc"] = t_y2
        prev_hp["xdd"] = t_yst
        prev_hp["cd2"] = em.tok("pe")
        prev_hp["dbufs"] = t_lt
        prev_hp["ltb"] = t_mt
        prev_hp["mtb"] = em.tok("pe")
        prev_hp["rf"] = em.tok("dve")
        prev_hp["zc"] = t_yst
        for it in range(NCP):
            prev_hp[("xtokr", it)] = em.tok("pe")
            prev_hp[("yfmr", it)] = t_yst
    em.op("sp", lambda e: None, deps=y_stores if y_stores else [t_wa])
    em.finalize()

    return nc



def build_addsum():
    nc = bass.Bass("TRN2", target_bir_lowering=False)
    x_in = nc.dram_tensor("x", [TOK, D], F32, kind="ExternalInput").ap()
    pa_in = nc.dram_tensor("pa", [TOK, D], BF16, kind="ExternalInput").ap()
    pb_in = nc.dram_tensor("pb", [TOK, D], BF16, kind="ExternalInput").ap()
    bo_in = nc.dram_tensor("bo", [128, D], F32, kind="ExternalInput").ap()
    xo = nc.dram_tensor("xo", [TOK, D], F32, kind="ExternalOutput").ap()
    em = Emitter(nc)
    xs = em.sbuf("xs", [128, 2 * D], F32)
    pas = em.sbuf("pas", [128, 2 * D], BF16)
    pbs = em.sbuf("pbs", [128, 2 * D], BF16)
    bos = em.sbuf("bos", [128, D], F32)
    t_bo = em.dma("sp", lambda e: e.dma_start(out=bos[:], in_=bo_in))
    prev = {0: None, 1: None}
    stores = []
    for i in range(NT):
        lb = i % 2
        dep = [prev[lb]] if prev[lb] else []
        t_x = em.dma("sp", lambda e, i=i, lb=lb: e.dma_start(
            out=xs[:, lb * D:(lb + 1) * D], in_=x_in[i * 128:(i + 1) * 128, :]), deps=dep)
        t_pa = em.dma("sp", lambda e, i=i, lb=lb: e.dma_start(
            out=pas[:, lb * D:(lb + 1) * D], in_=pa_in[i * 128:(i + 1) * 128, :]), deps=dep)
        t_pb = em.dma("sp", lambda e, i=i, lb=lb: e.dma_start(
            out=pbs[:, lb * D:(lb + 1) * D], in_=pb_in[i * 128:(i + 1) * 128, :]), deps=dep)
        xt = xs[:, lb * D:(lb + 1) * D]
        t1 = em.op("dve", lambda e, xt=xt, lb=lb: e.tensor_tensor(
            xt, xt, pas[:, lb * D:(lb + 1) * D], ALU.add), deps=[t_x, t_pa])
        t2 = em.op("dve", lambda e, xt=xt, lb=lb: e.tensor_tensor(
            xt, xt, pbs[:, lb * D:(lb + 1) * D], ALU.add), deps=[t1, t_pb])
        t3 = em.op("dve", lambda e, xt=xt: e.tensor_tensor(xt, xt, bos[:], ALU.add),
                   deps=[t2, t_bo])
        t_st = em.dma("sp", lambda e, i=i, lb=lb: e.dma_start(
            out=xo[i * 128:(i + 1) * 128, :], in_=xs[:, lb * D:(lb + 1) * D]), deps=[t3])
        prev[lb] = t_st
        stores.append(t_st)
    em.op("sp", lambda e: None, deps=stores)
    em.finalize()
    return nc


MOE_E = 8


def prep_moe_w(inp):
    eg = np.asarray(inp["L1_ffn_expert_gate"])
    eu = np.asarray(inp["L1_ffn_expert_up"])
    ed = np.asarray(inp["L1_ffn_expert_down"])
    sg = np.asarray(inp["L1_ffn_shared_gate"])
    su = np.asarray(inp["L1_ffn_shared_up"])
    sd = np.asarray(inp["L1_ffn_shared_down"])
    I = MOE_I
    wg = np.zeros((10, D, I), dtype=BF)
    wu = np.zeros((10, D, I), dtype=BF)
    wd = np.zeros((10, I, D), dtype=BF)
    for e in range(MOE_E):
        wg[e] = np_bf16(eg[e].T)
        wu[e] = np_bf16(eu[e].T)
        wd[e] = np_bf16(ed[e].T)
    wg[8] = np_bf16(sg[0:I].T); wg[9] = np_bf16(sg[I:2 * I].T)
    wu[8] = np_bf16(su[0:I].T); wu[9] = np_bf16(su[I:2 * I].T)
    sdT = np.asarray(sd).T
    wd[8] = np_bf16(sdT[0:I]); wd[9] = np_bf16(sdT[I:2 * I])
    return {"gwt": np_bf16(np.asarray(inp["L1_ffn_gate_w"]).T), "wg": wg, "wu": wu, "wd": wd,
            "ident": IDENT}


_NC_CACHE = {}


def _get_nc(name, builder):
    if name not in _NC_CACHE:
        _NC_CACHE[name] = builder()
    return _NC_CACHE[name]


def kernel(**inp):
    x = np.ascontiguousarray(np.asarray(inp["L0_attn_x"], dtype=np.float32).reshape(B * T, D))
    l0f = prep_l0ffn(inp)
    moew = prep_moe_w(inp)
    wq = np_bf16(inp["L0_attn_Wq"]); wk = np_bf16(inp["L0_attn_Wk"]); wv = np_bf16(inp["L0_attn_Wv"])
    wo = np_bf16(inp["L0_attn_Wo"])
    bo_rep = np.ascontiguousarray(np.tile(np.asarray(inp["L0_attn_bo"], np.float32)[None, :], (128, 1)))
    ipt = np_bf16(np.asarray(inp["L1_attn_in_proj_w"]).T)
    opt = np_bf16(np.asarray(inp["L1_attn_out_proj_w"]).T)
    cw = np.asarray(inp["L1_attn_conv_w"], np.float32).reshape(4096, 4)
    cb = np.asarray(inp["L1_attn_conv_b"], np.float32)
    alog = np.asarray(inp["L1_attn_A_log"], np.float32)
    dskip = np.asarray(inp["L1_attn_D_skip"], np.float32)
    dtbias = np.asarray(inp["L1_attn_dt_bias"], np.float32)
    tri = np.zeros((128, 128), np.float32)
    for p in range(128):
        tri[p, p + 1:] = -1e30
    maskd = np.zeros((128, 64), np.float32)
    for p in range(128):
        maskd[p, :p % 64] = -1e30
    idf = np.eye(128, dtype=np.float32)
    CORES = list(range(8))

    def tok_maps(xcur, extra):
        return [dict(extra, x=np.ascontiguousarray(xcur[c * TOK:(c + 1) * TOK])) for c in range(8)]

    res = run_bass_kernel_spmd(_get_nc("ffn", build_ffn_l0), tok_maps(x, l0f), core_ids=CORES)
    x1 = np.concatenate([np.asarray(r["xo"]) for r in res.results], axis=0)
    res = run_bass_kernel_spmd(_get_nc("qkv", build_qkv),
                               tok_maps(x1, {"wq": wq, "wk": wk, "wv": wv, "ident": IDENT}),
                               core_ids=CORES)
    Qf = [np.concatenate([np.asarray(res.results[b * 2 + h]["q"]) for h in range(2)], axis=1) for b in range(B)]
    Kf = [np.concatenate([np.asarray(res.results[b * 2 + h]["k"]) for h in range(2)], axis=1) for b in range(B)]
    Vt = [np.concatenate([np.asarray(res.results[b * 2 + h]["v"]) for h in range(2)], axis=0) for b in range(B)]
    amaps = []
    for c in range(8):
        b, gp = c // 2, c % 2
        amaps.append({
            "q": np.ascontiguousarray(Qf[b][gp * 1024:(gp + 1) * 1024]),
            "k": np.ascontiguousarray(Kf[b][gp * 256:(gp + 1) * 256]),
            "v": np.ascontiguousarray(Vt[b][:, gp * 256:(gp + 1) * 256]),
            "wo": np.ascontiguousarray(wo[gp * 1024:(gp + 1) * 1024]),
            "ident": IDENT, "trimask": tri,
        })
    res = run_bass_kernel_spmd(_get_nc("attn", build_attn), amaps, core_ids=CORES)
    parts = [np.asarray(r["part"]) for r in res.results]
    smaps = []
    for c in range(8):
        b, h = c // 2, c % 2
        smaps.append({"x": np.ascontiguousarray(x1[c * TOK:(c + 1) * TOK]),
                      "pa": np.ascontiguousarray(parts[b * 2][h * TOK:(h + 1) * TOK]),
                      "pb": np.ascontiguousarray(parts[b * 2 + 1][h * TOK:(h + 1) * TOK]),
                      "bo": bo_rep})
    res = run_bass_kernel_spmd(_get_nc("addsum", build_addsum), smaps, core_ids=CORES)
    x2 = np.concatenate([np.asarray(r["xo"]) for r in res.results], axis=0)
    res = run_bass_kernel_spmd(_get_nc("ffn", build_ffn_l0), tok_maps(x2, l0f), core_ids=CORES)
    x3 = np.concatenate([np.asarray(r["xo"]) for r in res.results], axis=0)
    res = run_bass_kernel_spmd(_get_nc("moe", build_moe), tok_maps(x3, moew), core_ids=CORES)
    x4 = np.concatenate([np.asarray(r["xo"]) for r in res.results], axis=0)
    res = run_bass_kernel_spmd(_get_nc("inproj", build_inproj),
                               tok_maps(x4, {"ipt": ipt, "ident": IDENT}), core_ids=CORES)
    xbs = [np.concatenate([np.asarray(res.results[b * 2 + h]["xb"]) for h in range(2)], axis=1) for b in range(B)]
    zs = [np.concatenate([np.asarray(res.results[b * 2 + h]["z"]) for h in range(2)], axis=1) for b in range(B)]
    bcs = [np.concatenate([np.asarray(res.results[b * 2 + h]["bc"]) for h in range(2)], axis=1) for b in range(B)]
    dts = [np.concatenate([np.asarray(res.results[b * 2 + h]["dt"]) for h in range(2)], axis=1) for b in range(B)]
    smaps = []
    for c in range(8):
        b, hf = c // 2, c % 2
        cwr = cw[hf * 2048:(hf + 1) * 2048]
        smaps.append({
            "xb": np.ascontiguousarray(xbs[b][hf * 2048:(hf + 1) * 2048]),
            "z": np.ascontiguousarray(zs[b][hf * 2048:(hf + 1) * 2048]),
            "bc": np.ascontiguousarray(bcs[b]),
            "dt": np.ascontiguousarray(dts[b][hf * 32:(hf + 1) * 32]),
            "alog": np.ascontiguousarray(alog[hf * 32:(hf + 1) * 32].reshape(32, 1)),
            "dtbias": np.ascontiguousarray(dtbias[hf * 32:(hf + 1) * 32].reshape(32, 1)),
            "convw": np.ascontiguousarray(cwr.reshape(16, 128, 4).transpose(1, 0, 2).reshape(128, 64)),
            "convb": np.ascontiguousarray(cb[hf * 2048:(hf + 1) * 2048].reshape(16, 128).T),
            "dskip": np.ascontiguousarray(dskip[hf * 2048:(hf + 1) * 2048].reshape(16, 128).T),
            "maskd": maskd, "ident": IDENT, "identf": idf,
        })
    res = run_bass_kernel_spmd(_get_nc("ssd", build_ssd), smaps, core_ids=CORES)
    ys = [np.concatenate([np.asarray(res.results[b * 2 + hf]["y"]) for hf in range(2)], axis=0) for b in range(B)]
    omaps = []
    for c in range(8):
        b, h = c // 2, c % 2
        ysl = ys[b][:, h * TOK:(h + 1) * TOK]
        ytiles = ysl.reshape(32, 128, NT, 128).transpose(2, 0, 1, 3)
        omaps.append({"x": np.ascontiguousarray(x4[c * TOK:(c + 1) * TOK]),
                      "yt": np.ascontiguousarray(ytiles), "opt": opt})
    res = run_bass_kernel_spmd(_get_nc("outproj", build_outproj), omaps, core_ids=CORES)
    x5 = np.concatenate([np.asarray(r["xo"]) for r in res.results], axis=0)
    res = run_bass_kernel_spmd(_get_nc("moe", build_moe), tok_maps(x5, moew), core_ids=CORES)
    x6 = np.concatenate([np.asarray(r["xo"]) for r in res.results], axis=0)
    return np.ascontiguousarray(x6.reshape(B, T, D))
